# version 01
# baseline (speedup 1.0000x reference)
import os
import numpy as np
import concourse.bass as bass
import concourse.mybir as mybir
from concourse.bass_utils import run_bass_kernel_spmd

F32 = mybir.dt.float32
F16 = mybir.dt.float16
BF16 = mybir.dt.bfloat16
AF = mybir.ActivationFunctionType
ALU = mybir.AluOpType

D = 2048
NT = 16
T = 2048
H = 16
SEQ = 8192
EPS = 1e-5
SCALE = float(1.0 / np.sqrt(128.0))

STAGE = 4
SUB = 99


class DSem:
    def __init__(self, name):
        self.name = name
        self.total = 0


class Op:
    __slots__ = ("eng", "fn", "waits", "seq", "needs_inc", "dsem", "dtarget",
                 "dinc", "tag")

    def __init__(self, eng, fn, seq, tag):
        self.eng, self.fn, self.seq, self.tag = eng, fn, seq, tag
        self.needs_inc = False
        self.dsem = None
        self.dtarget = 0
        self.waits = []


class Prog:
    ENGINES = ("tensor", "vector", "scalar", "gpsimd", "sync")

    def __init__(self, nc):
        self.nc = nc
        self.ops = {e: [] for e in self.ENGINES}
        self.res = {}
        self.dsems = {}
        self.strict = bool(os.environ.get("KSTRICT"))

    def dsem(self, name):
        if name not in self.dsems:
            self.dsems[name] = DSem(name)
        return self.dsems[name]

    def _r(self, n):
        if n not in self.res:
            self.res[n] = {"w": None, "r": []}
        return self.res[n]

    def add(self, eng, fn, reads=(), writes=(), dsem=None, dinc=16, tag=""):
        op = Op(eng, fn, len(self.ops[eng]), tag)
        deps, seen = [], set()
        for rn in reads:
            r = self._r(rn)
            if r["w"] is not None:
                deps.append(r["w"])
        for wn in writes:
            r = self._r(wn)
            if r["w"] is not None:
                deps.append(r["w"])
            deps.extend(r["r"])
        for d in deps:
            if id(d) in seen or d is op:
                continue
            seen.add(id(d))
            if d.dsem is not None:
                op.waits.append(("d", d.dsem, d.dtarget))
            elif d.eng != eng or self.strict:
                d.needs_inc = True
                op.waits.append(d)
        if dsem is not None:
            ds = self.dsem(dsem)
            ds.total += dinc
            op.dsem = ds
            op.dtarget = ds.total
            op.dinc = dinc
        for rn in reads:
            self._r(rn)["r"].append(op)
        for wn in writes:
            r = self._r(wn)
            r["w"] = op
            r["r"] = []
        self.ops[eng].append(op)
        return op

    def emit(self):
        nc = self.nc
        prefix = {}
        for e in self.ENGINES:
            cnt, pre = 0, []
            for op in self.ops[e]:
                if op.needs_inc:
                    cnt += 1
                pre.append(cnt)
            prefix[e] = pre
        sems, ctxs = {}, []
        for e in self.ENGINES:
            cm = nc.semaphore(f"se_{e}")
            sems[e] = cm.__enter__()
            ctxs.append(cm)
        for name, ds in self.dsems.items():
            cm = nc.semaphore(f"sd_{name}")
            sems[id(ds)] = cm.__enter__()
            ctxs.append(cm)

        blk_cm = nc.Block()
        block = blk_cm.__enter__()

        def body(en):
            def run(eng):
                hi = {}
                for op in self.ops[en]:
                    for w in op.waits:
                        if isinstance(w, tuple):
                            _, ds, val = w
                            key = id(ds)
                            if val > hi.get(key, 0):
                                hi[key] = val
                                eng.wait_ge(sems[key], val)
                        else:
                            val = prefix[w.eng][w.seq]
                            if val > hi.get(w.eng, 0):
                                hi[w.eng] = val
                                eng.wait_ge(sems[w.eng], val)
                    inst = op.fn(eng)
                    if op.dsem is not None:
                        inst.then_inc(sems[id(op.dsem)], op.dinc)
                    elif op.needs_inc:
                        inst.then_inc(sems[op.eng], 1)
            return run

        if self.ops["tensor"]:
            block.tensor(body("tensor"))
        if self.ops["vector"]:
            block.vector(body("vector"))
        if self.ops["scalar"]:
            block.scalar(body("scalar"))
        if self.ops["gpsimd"]:
            block.gpsimd(body("gpsimd"))
        if self.ops["sync"]:
            block.sync(body("sync"))
        blk_cm.__exit__(None, None, None)
        for cm in ctxs:
            cm.__exit__(None, None, None)


def gr(region, lo, hi):
    return [f"{region}{i}" for i in range(lo // 2048, (hi + 2047) // 2048)]


def build(stage=STAGE):
    nc = bass.Bass("TRN2", target_bir_lowering=False,
                   dynamic_dma_scratch_size=2048,
                   detect_race_conditions=not os.environ.get("KNORACE"))

    def din(name, shape, dtype=F32):
        return nc.dram_tensor(name, list(shape), dtype, kind="ExternalInput")

    x_fm = din("x_fm", (D, T))
    wqk = din("wqk", (D, D))
    wv = din("wv", (D, D))
    wu = din("wu", (D, D))
    wo = din("wo", (D, D))
    ffnw = din("ffnw", (D, D))
    wq = din("wq", (D, D))
    wkv = din("wkv", (D, 128))
    projw = din("projw", (D, D))
    kexp = din("kexp", (128, H * 64))
    vexp = din("vexp", (64, H * 128))
    biasp = din("biasp", (128, 128))
    bvrow = din("bvrow", (1, D))
    ident_in = din("ident_in", (128, 128))
    p128_in = din("p128_in", (128, 128))
    ropeq_c = din("ropeq_c", (128, T))
    ropeq_s = din("ropeq_s", (128, T))
    ropek_in = din("ropek_in", (128, T))
    masks_in = din("masks_in", (128, 4 * 512))

    out_fm = nc.dram_tensor("out_fm", [D, T], F32, kind="ExternalOutput")
    out_dbg = nc.dram_tensor("out_dbg", [128, D], F32, kind="ExternalOutput")

    ar_in = nc.dram_tensor("ar_in", [128, D], F32, kind="Internal")
    ar_out = nc.dram_tensor("ar_out", [128, D], F32, kind="Internal")
    ag_in = nc.dram_tensor("ag_in", [128, D], F16, kind="Internal")
    ag_out = nc.dram_tensor("ag_out", [512, D], F16, kind="Internal")

    A = nc.alloc_sbuf_tensor
    RES = A("RES", [128, NT * T], F16)
    ABUF = A("ABUF", [128, NT * T], F16)
    POOL = A("POOL", [128, 24576], F16)
    ab_addr = nc.lookup_mloc(ABUF).addr
    pool_addr = nc.lookup_mloc(POOL).addr

    def ov(name, shape, dtype, base, off):
        return nc.alloc_sbuf_tensor_at(name, list(shape), dtype, offset=base + off)

    qt_all = ov("qt_all", (128, 16384), F16, ab_addr, 0)
    k_all = ov("k_all", (128, SEQ), F16, ab_addr, 32768)
    v_tok = ov("v_tok", (128, 64 * 65), BF16, ab_addr, 49152)
    attnT = [ov(f"attnT{i}", (128, 1024), BF16, ab_addr, 57472 + i * 2048)
             for i in range(2)]
    rq_c = ov("rq_c", (128, T), F16, ab_addr, 32768)
    rq_s = ov("rq_s", (128, T), F16, ab_addr, 36864)
    rk_cs = ov("rk_cs", (128, T), F16, ab_addr, 40960)
    AB_QT = gr("ab", 0, 32768)
    AB_KALL = gr("ab", 32768, 49152)
    AB_VTOK = gr("ab", 49152, 57472)
    AB_AT = [gr("ab", 57472, 59520), gr("ab", 59520, 61568)]
    AB_RQC = gr("ab", 32768, 36864)
    AB_RQS = gr("ab", 36864, 40960)
    AB_RKCS = gr("ab", 40960, 45056)

    def ab_tile(h):
        return gr("ab", h * T * 2, (h + 1) * T * 2)

    def ab_chunk(h, fc):
        return gr("ab", (h * T + fc * 512) * 2, (h * T + (fc + 1) * 512) * 2)

    vt_tiles = ov("vt_tiles", (128, 8 * T), F16, pool_addr, 0)
    a_tok = ov("a_tok", (128, 1024), F16, pool_addr, 32768)
    vv_tok = ov("vv_tok", (128, 1024), F16, pool_addr, 34816)
    kv16 = ov("kv16", (128, T), F16, pool_addr, 36864)
    rows16b = ov("rows16b", (128, T), F16, pool_addr, 40960)
    bv_sb = ov("bv_sb", (1, D), F16, pool_addr, 45056)
    masks = ov("masks", (128, 4 * 512), BF16, pool_addr, 0)
    kexp16 = ov("kexp16", (128, H * 64), F16, pool_addr, 4096)
    vexp16 = ov("vexp16", (64, H * 128), F16, pool_addr, 6144)
    y_sb = ov("y_sb", (64, 1024), F32, pool_addr, 10240)
    y_norm = ov("y_norm", (64, 1024), F16, pool_addr, 14336)
    ybuf = ov("ybuf", (128, NT * 512), F16, pool_addr, 16384)
    qraw = ov("qraw", (128, T), F16, pool_addr, 32768)
    agbuf = ov("agbuf", (128, D), F16, pool_addr, 40960)
    gcol = ov("gcol", (128, 16), F32, pool_addr, 45056)
    PL_VT = lambda pt: gr("pl", (pt % 8) * T * 2, ((pt % 8) + 1) * T * 2)
    PL_ATOK = gr("pl", 32768, 34816)
    PL_VVTOK = gr("pl", 34816, 36864)
    PL_KV16 = gr("pl", 36864, 40960)
    PL_R16B = gr("pl", 40960, 45056)
    PL_BV = gr("pl", 45056, 49152)
    PL_MASK = gr("pl", 0, 4096)
    PL_KEXP = gr("pl", 4096, 6144)
    PL_VEXP = gr("pl", 6144, 10240)
    PL_YSB = gr("pl", 10240, 14336)
    PL_YN = gr("pl", 14336, 16384)
    PL_YBUF = gr("pl", 16384, 32768)
    PL_QRAW = gr("pl", 32768, 36864)
    PL_AG = gr("pl", 40960, 45056)
    PL_GCOL = gr("pl", 45056, 45120)

    ident16 = A("ident16", [128, 128], F16)
    p128 = A("p128", [128, 128], F16)
    ones_l16 = A("ones_l16", [128, 128], F16)
    ones_l32 = A("ones_l32", [128, 128], F32)
    ones_c16 = A("ones_c16", [128, 1], F16)
    neg1c = A("neg1c", [128, 1], F32)
    epsc = A("epsc", [1, 1], F32)
    bias_sb = A("bias_sb", [128, 128], F32)
    rows16 = A("rows16", [128, T], F16)
    row32 = A("row32", [128, T], F32)
    is_bc = A("is_bc", [128, T], F16)
    mis_bc = A("mis_bc", [128, T], F16)
    wstage = [A(f"wstage{i}", [128, 1024], F32) for i in range(2)]
    wblk = [A(f"wblk{i}", [128, T], F16) for i in range(2)]
    scr1 = A("scr1", [128, T], F16)
    scr2 = A("scr2", [128, 1024], F16)
    wsum_sb = A("wsum_sb", [128, 2], F32)
    kmaxc = A("kmaxc", [1, 4], F32)
    m2row = A("m2row", [1, 16], F32)
    m2tmp = A("m2tmp", [1, 16], F32)

    assert nc.sbuf_bytes_remaining >= 0

    PROJ = [nc.alloc_psum_tensor(f"PROJ{i}", [128, T], F32) for i in range(2)]
    psave = nc.psum_base
    nc.psum_base = 0
    P0 = nc.alloc_psum_tensor("P0", [128, 1024], F32)
    P1 = nc.alloc_psum_tensor("P1", [128, 1024], F32)
    YB = nc.alloc_psum_tensor("YB", [128, 1024], F32)
    SP = nc.alloc_psum_tensor("SP", [128, 1024], F32)
    nc.psum_base = 0
    PT16 = nc.alloc_psum_tensor("PT16", [128, 1024], F16)
    nc.psum_base = 0
    SUMP = nc.alloc_psum_tensor("SUMP", [1, T], F32)
    SQP = nc.alloc_psum_tensor("SQP", [1, T], F32)
    nc.psum_base = psave

    pb = [f"pb{i}" for i in range(8)]
    PROJ_R = [pb[0:4], pb[4:8]]
    P0_R, P1_R, YB_R, SP_R = pb[0:2], pb[2:4], pb[4:6], pb[6:8]
    PT16_R = pb[0:1]
    SUMP_R, SQP_R = pb[0:4], pb[4:8]
    PSC = [P0, P1]
    PSC_R = [P0_R, P1_R]

    p = Prog(nc)

    def MM(o, l, r, start, stop, skip=False):
        return (lambda e, o=o, l=l, r=r, s=start, t=stop, sk=skip:
                e.matmul(o, l, r, start=s, stop=t, skip_group_check=sk))

    def seq_fns(fns):
        def f(e):
            last = None
            for fn in fns:
                last = fn(e)
            return last
        return f

    BQK, BV_, BU, BO, FFB, FFS, LNG, LNB = range(8)

    def bcol(v, pt):
        return bias_sb[:, v * 16 + pt: v * 16 + pt + 1]

    _noconst = os.environ.get("KNOCONST")
    _noin = os.environ.get("KNOIN")
    _nost = os.environ.get("KNOST")
    def ld16(dst, dst_res, srcap, nm, cols, rows=128):
        p.add("sync", (lambda e, srcap=srcap: e.dma_start(out=wstage[0][0:rows, 0:cols],
                                                          in_=srcap)),
              writes=["wst0"], dsem="wsd0", tag=f"ldc_{nm}")
        p.add("vector", (lambda e, dst=dst: e.tensor_copy(dst, wstage[0][0:rows, 0:cols])),
              reads=["wst0"], writes=dst_res if isinstance(dst_res, list) else [dst_res],
              tag=f"cst_{nm}")

    if not _noconst:
        ld16(ident16[:, :], "ident16", ident_in.ap(), "id", 128)
        ld16(p128[:, :], "p128", p128_in.ap(), "p128", 128)
        p.add("sync", lambda e: e.dma_start(out=bias_sb[:, :], in_=biasp.ap()),
              writes=["bias_sb"], dsem="c3")
        ld16(bv_sb[0:1, 0:1024], PL_BV, bvrow[0:1, 0:1024], "bv0", 1024, rows=1)
        ld16(bv_sb[0:1, 1024:2048], PL_BV, bvrow[0:1, 1024:2048], "bv1", 1024, rows=1)
    if not _noconst:
      for t, w in [(ones_l16, "ones_l16"), (ones_l32, "ones_l32"),
                   (ones_c16, "ones_c16"), (neg1c, "neg1c")]:
          p.add("vector", (lambda e, t=t, v=(1.0 if w != "neg1c" else -1.0):
                           e.memset(t[:, :], v)), writes=[w])
      p.add("vector", lambda e: e.memset(epsc[:, :], EPS), writes=["epsc"])

    for i in (range(32) if not _noin else []):
        ft, h2 = i // 2, i % 2
        sl = i % 2
        p.add("sync",
              (lambda e, ft=ft, h2=h2, s=wstage[sl]:
               e.dma_start(out=s[:, :], in_=x_fm[ft * 128:(ft + 1) * 128,
                                                 h2 * 1024:(h2 + 1) * 1024])),
              writes=[f"wst{sl}"], dsem=f"wsd{sl}", tag=f"xld{i}")
        p.add("vector",
              (lambda e, ft=ft, h2=h2, s=wstage[sl]:
               e.tensor_copy(RES[:, ft * T + h2 * 1024: ft * T + (h2 + 1) * 1024],
                             s[:, :])),
              reads=[f"wst{sl}"], writes=[f"res{ft}"], tag=f"xcast{i}")

    def ln_stats(tagp, src, src_res, bc_mu=False, gran_fn=None):
        for ft in range(16):
            rres = src_res(ft)
            for hh in range(2):
                p.add("scalar",
                      (lambda e, ft=ft, hh=hh:
                       e.activation(scr1[:, hh * 1024:(hh + 1) * 1024],
                                    src[:, ft * T + hh * 1024: ft * T + (hh + 1) * 1024],
                                    AF.Square)),
                      reads=rres, writes=[f"scr1h{hh}"], tag=f"{tagp}sq{ft}{hh}")
            fns = [MM(SUMP[0:1, fc * 512:(fc + 1) * 512], ones_c16[:, :],
                      src[:, ft * T + fc * 512: ft * T + (fc + 1) * 512],
                      start=(ft == 0), stop=(ft == 15)) for fc in range(4)]
            p.add("tensor", seq_fns(fns), reads=rres + ["ones_c16"],
                  writes=SUMP_R, tag=f"{tagp}sum{ft}")
            fns = [MM(SQP[0:1, fc * 512:(fc + 1) * 512], ones_c16[:, :],
                      scr1[:, fc * 512:(fc + 1) * 512],
                      start=(ft == 0), stop=(ft == 15)) for fc in range(4)]
            p.add("tensor", seq_fns(fns), reads=["scr1h0", "scr1h1", "ones_c16"],
                  writes=SQP_R, tag=f"{tagp}ssq{ft}")
        p.add("vector", lambda e: e.tensor_scalar_mul(row32[0:1, :], SUMP[0:1, :], 1.0 / D),
              reads=SUMP_R, writes=["rw0"], tag=f"{tagp}mu")
        p.add("vector", lambda e: e.tensor_scalar_mul(row32[32:33, :], SQP[0:1, :], 1.0 / D),
              reads=SQP_R, writes=["rw32"], tag=f"{tagp}ex2")
        p.add("vector", lambda e: e.tensor_mul(SUMP[0:1, :], row32[0:1, :], row32[0:1, :]),
              reads=["rw0"], writes=SUMP_R, tag=f"{tagp}m2")
        p.add("vector", lambda e: e.tensor_sub(row32[32:33, :], row32[32:33, :],
                                               SUMP[0:1, :]),
              reads=["rw32"] + SUMP_R, writes=["rw32"], tag=f"{tagp}var")
        p.add("scalar", lambda e: e.activation(row32[32:33, :], row32[32:33, :], AF.Sqrt,
                                               bias=epsc[0:1, 0:1]),
              reads=["rw32", "epsc"], writes=["rw32"], tag=f"{tagp}sd")
        p.add("vector", lambda e: e.reciprocal(row32[64:65, :], row32[32:33, :]),
              reads=["rw32"], writes=["rw64"], tag=f"{tagp}is")
        rdst = rows16 if not bc_mu else rows16b
        rr = "rows16" if not bc_mu else "r16b"
        p.add("vector", (lambda e, rdst=rdst: e.tensor_copy(rdst[0:1, :], row32[0:1, :])),
              reads=["rw0"], writes=[rr])
        p.add("vector", (lambda e, rdst=rdst: e.tensor_copy(rdst[32:33, :], row32[64:65, :])),
              reads=["rw64"], writes=[rr])

    def build_bc(tagp, rsrc, rres, mult_mis):
        for half in range(2):
            fns = [MM(SP[:, fc * 512:(fc + 1) * 512], ones_l16[32:33, :],
                      rsrc[32:33, half * 1024 + fc * 512: half * 1024 + (fc + 1) * 512],
                      start=True, stop=True) for fc in range(2)]
            p.add("tensor", seq_fns(fns), reads=[rres, "ones_l16"], writes=SP_R,
                  tag=f"{tagp}bci{half}")
            p.add("vector",
                  (lambda e, half=half:
                   e.tensor_copy(is_bc[:, half * 1024:(half + 1) * 1024], SP[:, :])),
                  reads=SP_R, writes=["isbc"], tag=f"{tagp}bciv{half}")
        for half in range(2):
            fns = [MM(SP[:, fc * 512:(fc + 1) * 512], ones_l16[0:1, :],
                      rsrc[0:1, half * 1024 + fc * 512: half * 1024 + (fc + 1) * 512],
                      start=True, stop=True) for fc in range(2)]
            p.add("tensor", seq_fns(fns), reads=[rres, "ones_l16"], writes=SP_R,
                  tag=f"{tagp}bcm{half}")
            if mult_mis:
                p.add("vector",
                      (lambda e, half=half:
                       e.tensor_mul(mis_bc[:, half * 1024:(half + 1) * 1024], SP[:, :],
                                    is_bc[:, half * 1024:(half + 1) * 1024])),
                      reads=SP_R + ["isbc"], writes=["misbc"], tag=f"{tagp}bcmv{half}")
            else:
                p.add("vector",
                      (lambda e, half=half:
                       e.tensor_copy(mis_bc[:, half * 1024:(half + 1) * 1024], SP[:, :])),
                      reads=SP_R, writes=["misbc"], tag=f"{tagp}bcmv{half}")

    wcnt = [0]

    def w_proj(w_dram, tagp, epi_cb, pts=range(16), nct=16, wsum=False,
               bias_row=False):
        ncols = w_dram.shape[1]
        for pn, pt in enumerate(pts):
            i = wcnt[0]
            for h2 in range(2):
                sl = (2 * i + h2) % 2
                p.add("sync",
                      (lambda e, pt=pt, h2=h2, s=wstage[sl], w=w_dram, ncols=ncols:
                       e.dma_start(
                           out=s[:, :].rearrange("p (c j) -> p c j", c=8),
                           in_=bass.AP(w, pt * 128 + h2 * 8 * 128 * ncols,
                                       [[ncols, 128], [128 * ncols, 8], [1, 128]]))),
                      writes=[f"wst{sl}"], dsem=f"wsd{sl}", tag=f"{tagp}wd{pt}_{h2}")
                p.add("vector",
                      (lambda e, h2=h2, s=wstage[sl], d=wblk[i % 2]:
                       e.tensor_copy(d[:, h2 * 1024:(h2 + 1) * 1024], s[:, :])),
                      reads=[f"wst{sl}"], writes=[f"wblk{i % 2}"],
                      tag=f"{tagp}wc{pt}_{h2}")
            ps, ps_r = PROJ[i % 2], PROJ_R[i % 2]
            if wsum:
                fns = [MM(ps[:, 0:1], wblk[i % 2][:, ct * 128:(ct + 1) * 128],
                          ones_c16[:, 0:1], start=(ct == 0), stop=(ct == nct - 1))
                       for ct in range(nct)]
                p.add("tensor", seq_fns(fns), reads=[f"wblk{i % 2}", "ones_c16"],
                      writes=ps_r, tag=f"{tagp}ws{pt}")
                p.add("vector",
                      (lambda e, ps=ps: e.tensor_scalar_mul(wsum_sb[:, 0:1],
                                                            ps[:, 0:1], -1.0)),
                      reads=ps_r, writes=["wsum_sb"], tag=f"{tagp}wse{pt}")
            fns = []
            last = (lambda ct: ct == nct - 1) if not bias_row else (lambda ct: False)
            for ct in range(nct):
                for fc in range(4):
                    fns.append(MM(ps[:, fc * 512:(fc + 1) * 512],
                                  wblk[i % 2][:, ct * 128:(ct + 1) * 128],
                                  RES[:, ct * T + fc * 512: ct * T + (fc + 1) * 512],
                                  start=(ct == 0), stop=last(ct)))
            reads = [f"wblk{i % 2}"] + [f"res{ct}" for ct in range(nct)]
            if wsum:
                reads.append("wsum_sb")
            p.add("tensor", seq_fns(fns), reads=reads, writes=ps_r,
                  tag=f"{tagp}mm{pt}")
            epi_cb(pn, pt, i, ps, ps_r)
            wcnt[0] += 1

    def lnfold_chunk(tagp, pt, fc, ps, ps_r):
        s = slice(fc * 512, (fc + 1) * 512)
        p.add("vector",
              (lambda e, ps=ps, s=s: e.tensor_mul(scr1[:, s], ps[:, s], is_bc[:, s])),
              reads=ps_r + ["isbc"], writes=[f"scr1h{fc // 2}"], tag=f"{tagp}t1_{pt}_{fc}")
        p.add("vector",
              (lambda e, s=s: e.scalar_tensor_tensor(scr1[:, s], mis_bc[:, s],
                                                     wsum_sb[:, 0:1], scr1[:, s],
                                                     ALU.mult, ALU.add)),
              reads=["misbc", "wsum_sb", f"scr1h{fc // 2}"], writes=[f"scr1h{fc // 2}"],
              tag=f"{tagp}t2_{pt}_{fc}")

    sub = SUB
    ks = os.environ.get("KSUB")
    if ks is not None:
        ks = set(ks.split(","))
        class _S:
            def __ge__(self, n):
                return {2: "ln", 3: "a", 4: "v", 5: "cc", 5.5: "kvb", 6: "u",
                        7: "out", 8: "gate", 9: "wo"}[n] in ks
        sub = _S()
    if sub >= 2:
        ln_stats("l1", RES, lambda ft: [f"res{ft}"])
        build_bc("l1", rows16, "rows16", True)

    def a_epi(pn, pt, i, ps, ps_r):
        for fc in range(4):
            s = slice(fc * 512, (fc + 1) * 512)
            d = slice(pt * T + fc * 512, pt * T + (fc + 1) * 512)
            lnfold_chunk("a", pt, fc, ps, ps_r)
            p.add("gpsimd",
                  (lambda e, s=s, pt=pt:
                   e.tensor_scalar(scr2[:, 0:512], scr1[:, s], bcol(BQK, pt), 0.0,
                                   ALU.add, ALU.min)),
                  reads=[f"scr1h{fc // 2}", "bias_sb"], writes=["scr2a"],
                  tag=f"a_xm{pt}_{fc}")
            p.add("scalar",
                  (lambda e: e.activation(scr2[:, 0:512], scr2[:, 0:512], AF.Exp)),
                  reads=["scr2a"], writes=["scr2a"], tag=f"a_em{pt}_{fc}")
            p.add("gpsimd",
                  (lambda e, s=s, pt=pt:
                   e.tensor_scalar(scr1[:, s], scr1[:, s], bcol(BQK, pt), 0.0,
                                   ALU.add, ALU.max)),
                  reads=[f"scr1h{fc // 2}", "bias_sb"], writes=[f"scr1h{fc // 2}"],
                  tag=f"a_xp{pt}_{fc}")
            p.add("vector",
                  (lambda e, s=s, d=d:
                   e.scalar_tensor_tensor(ABUF[:, d], scr1[:, s], neg1c[:, 0:1],
                                          scr2[:, 0:512], ALU.add, ALU.add)),
                  reads=[f"scr1h{fc // 2}", "scr2a", "neg1c"], writes=ab_chunk(pt, fc),
                  tag=f"a_o{pt}_{fc}")

    if sub >= 3:
        w_proj(wqk, "a", a_epi, wsum=True)

    def v_epi(pn, pt, i, ps, ps_r):
        gslot = pt % 8
        for fc in range(4):
            s = slice(fc * 512, (fc + 1) * 512)
            d = slice(gslot * T + fc * 512, gslot * T + (fc + 1) * 512)
            lnfold_chunk("v", pt, fc, ps, ps_r)
            p.add("scalar",
                  (lambda e, s=s, d=d, pt=pt:
                   e.activation(vt_tiles[:, d], scr1[:, s], AF.Silu,
                                bias=bcol(BV_, pt))),
                  reads=[f"scr1h{fc // 2}", "bias_sb"], writes=PL_VT(pt),
                  tag=f"v_o{pt}_{fc}")

    def kv_group(grp):
        heads = list(range(grp * 8, grp * 8 + 8))
        for nt in range(16):
            fns = [(lambda e, h=h, nt=nt, j=j:
                    e.transpose(PT16[:, j * 128:(j + 1) * 128],
                                ABUF[:, h * T + nt * 128: h * T + (nt + 1) * 128],
                                ident16[:, :]))
                   for j, h in enumerate(heads)]
            rd = sum([ab_tile(h) for h in heads], []) + ["ident16"]
            p.add("tensor", seq_fns(fns), reads=rd, writes=PT16_R,
                  tag=f"kv{grp}tA{nt}")
            p.add("vector", lambda e: e.tensor_copy(a_tok[:, :], PT16[:, :]),
                  reads=PT16_R, writes=PL_ATOK, tag=f"kv{grp}eA{nt}")
            fns = [(lambda e, h=h, nt=nt, j=j, grp=grp:
                    e.transpose(PT16[:, j * 128:(j + 1) * 128],
                                vt_tiles[:, (h - grp * 8) * T + nt * 128:
                                         (h - grp * 8) * T + (nt + 1) * 128],
                                ident16[:, :]))
                   for j, h in enumerate(heads)]
            rd = sum([PL_VT(h) for h in heads], []) + ["ident16", ] + PL_ATOK
            p.add("tensor", seq_fns(fns), reads=rd, writes=PT16_R,
                  tag=f"kv{grp}tV{nt}")
            p.add("vector", lambda e: e.tensor_copy(vv_tok[:, :], PT16[:, :]),
                  reads=PT16_R, writes=PL_VVTOK, tag=f"kv{grp}eV{nt}")
            fns = [MM(PROJ[1][:, j * 128:(j + 1) * 128],
                      a_tok[:, j * 128:(j + 1) * 128],
                      vv_tok[:, j * 128:(j + 1) * 128],
                      start=(nt == 0 and j % 4 == 0), stop=(nt == 15), skip=True)
                   for j in range(8)]
            p.add("tensor", seq_fns(fns), reads=PL_ATOK + PL_VVTOK,
                  writes=PROJ_R[1], tag=f"kv{grp}mm{nt}")
        for half in range(2):
            sl = half
            p.add("vector",
                  (lambda e, half=half, s=wstage[sl]:
                   e.tensor_copy(s[:, 0:512], PROJ[1][:, half * 512:(half + 1) * 512])),
                  reads=PROJ_R[1], writes=[f"wst{sl}"], tag=f"kv{grp}ev{half}")
            p.add("gpsimd",
                  (lambda e, grp=grp, half=half, s=wstage[sl]:
                   e.dma_start(out=ar_in[:, grp * 1024 + half * 512:
                                         grp * 1024 + (half + 1) * 512],
                               in_=s[:, 0:512])),
                  reads=[f"wst{sl}"], writes=["ar_in"], dsem=f"arst{half}",
                  tag=f"kv{grp}st{half}")

    if sub >= 4:
        w_proj(wv, "v0", v_epi, pts=range(0, 8), wsum=True)
        kv_group(0)
        w_proj(wv, "v1", v_epi, pts=range(8, 16), wsum=True)
        kv_group(1)

    _mockcc = os.environ.get("KMOCKCC")
    if sub >= 5:
        if _mockcc:
            p.add("gpsimd", lambda e: e.dma_start(out=ar_out.ap(), in_=ar_in.ap()),
                  reads=["ar_in"], writes=["ar_out"], dsem="cc1", tag="allreduce_mock")
        else:
            p.add("gpsimd",
                  (lambda e: e.collective_compute(
                      "AllReduce", ALU.add, replica_groups=[[0, 1, 2, 3], [4, 5, 6, 7]],
                      ins=[ar_in.ap().opt()], outs=[ar_out.ap().opt()])),
                  reads=["ar_in"], writes=["ar_out"], dsem="cc1", dinc=1, tag="allreduce")
    for half in (range(2) if sub >= 5.5 else []):
        sl = half
        p.add("sync",
              (lambda e, half=half, s=wstage[sl]:
               e.dma_start(out=s[:, :], in_=ar_out[:, half * 1024:(half + 1) * 1024])),
              reads=["ar_out"], writes=[f"wst{sl}"], dsem=f"wsd{sl}", tag=f"kvb{half}")
        p.add("vector",
              (lambda e, half=half, s=wstage[sl]:
               e.tensor_copy(kv16[:, half * 1024:(half + 1) * 1024], s[:, :])),
              reads=[f"wst{sl}"], writes=PL_KV16, tag=f"kvc{half}")

    def u_epi(pn, pt, i, ps, ps_r):
        gslot = pt % 8
        for fc in range(4):
            s = slice(fc * 512, (fc + 1) * 512)
            d = slice(gslot * T + fc * 512, gslot * T + (fc + 1) * 512)
            lnfold_chunk("u", pt, fc, ps, ps_r)
            p.add("scalar",
                  (lambda e, s=s, d=d, pt=pt:
                   e.activation(vt_tiles[:, d], scr1[:, s], AF.Silu,
                                bias=bcol(BU, pt))),
                  reads=[f"scr1h{fc // 2}", "bias_sb"], writes=PL_VT(pt),
                  tag=f"u_o{pt}_{fc}")

    if sub >= 6:
        w_proj(wu, "u0", u_epi, pts=range(0, 8), wsum=True)

    for h in (range(16) if sub >= 7 else []):
        ps, ps_r = PROJ[h % 2], PROJ_R[h % 2]
        fns = [MM(ps[:, fc * 512:(fc + 1) * 512], kv16[:, h * 128:(h + 1) * 128],
                  ABUF[:, h * T + fc * 512: h * T + (fc + 1) * 512],
                  start=True, stop=True) for fc in range(4)]
        p.add("tensor", seq_fns(fns), reads=PL_KV16 + ab_tile(h), writes=ps_r,
              tag=f"o_mm{h}")
        p.add("vector",
              (lambda e, h=h, ps=ps:
               e.tensor_scalar_mul(ABUF[:, h * T:(h + 1) * T], ps[:, :], 1.0 / 512)),
              reads=ps_r, writes=ab_tile(h), tag=f"o_ev{h}")

    if sub >= 8:
        ln_stats("l2", ABUF, lambda ft: ab_tile(ft), bc_mu=True)
        build_bc("li", rows16b, "r16b", False)

    def gate(h, uslot):
        for fc in range(4):
            s = slice(fc * 512, (fc + 1) * 512)
            d = slice(h * T + fc * 512, h * T + (fc + 1) * 512)
            du = slice(uslot * T + fc * 512, uslot * T + (fc + 1) * 512)
            p.add("vector",
                  (lambda e, s=s, d=d: e.tensor_sub(scr1[:, s], ABUF[:, d],
                                                    mis_bc[:, s])),
                  reads=ab_chunk(h, fc) + ["misbc"], writes=[f"scr1h{fc // 2}"],
                  tag=f"g_a{h}_{fc}")
            p.add("vector",
                  (lambda e, s=s: e.tensor_mul(scr1[:, s], scr1[:, s], is_bc[:, s])),
                  reads=[f"scr1h{fc // 2}", "isbc"], writes=[f"scr1h{fc // 2}"],
                  tag=f"g_b{h}_{fc}")
            p.add("scalar",
                  (lambda e, s=s, h=h: e.activation(scr1[:, s], scr1[:, s], AF.Identity,
                                                    bias=bcol(LNB, h),
                                                    scale=bcol(LNG, h))),
                  reads=[f"scr1h{fc // 2}", "bias_sb"], writes=[f"scr1h{fc // 2}"],
                  tag=f"g_c{h}_{fc}")
            p.add("vector",
                  (lambda e, s=s, d=d, du=du:
                   e.tensor_mul(ABUF[:, d], scr1[:, s], vt_tiles[:, du])),
                  reads=[f"scr1h{fc // 2}"] + PL_VT(uslot), writes=ab_chunk(h, fc),
                  tag=f"g_o{h}_{fc}")

    if sub >= 8:
        for h in range(8):
            gate(h, h)
        build_bc("l1r", rows16, "rows16", True)
        w_proj(wu, "u1", u_epi, pts=range(8, 16), wsum=True)
        build_bc("lir", rows16b, "r16b", False)
        for h in range(8, 16):
            gate(h, h - 8)

    def wo_epi(pn, pt, i, ps, ps_r):
        for fc in range(4):
            s = slice(fc * 512, (fc + 1) * 512)
            d = slice(pt * T + fc * 512, pt * T + (fc + 1) * 512)
            p.add("vector",
                  (lambda e, s=s, d=d, pt=pt:
                   e.scalar_tensor_tensor(RES[:, d], ps[:, s], bcol(BO, pt),
                                          RES[:, d], ALU.add, ALU.add)),
                  reads=ps_r + ["bias_sb", f"res{pt}"], writes=[f"res{pt}"],
                  tag=f"wo_o{pt}_{fc}")

    def w_proj_src(w_dram, tagp, epi_cb, src, src_res, pts=range(16)):
        for pn, pt in enumerate(pts):
            i = wcnt[0]
            for h2 in range(2):
                sl = (2 * i + h2) % 2
                p.add("sync",
                      (lambda e, pt=pt, h2=h2, s=wstage[sl], w=w_dram:
                       e.dma_start(
                           out=s[:, :].rearrange("p (c j) -> p c j", c=8),
                           in_=bass.AP(w, pt * 128 + h2 * 8 * 128 * w.shape[1],
                                       [[w.shape[1], 128], [128 * w.shape[1], 8],
                                        [1, 128]]))),
                      writes=[f"wst{sl}"], dsem=f"wsd{sl}", tag=f"{tagp}wd{pt}_{h2}")
                p.add("vector",
                      (lambda e, h2=h2, s=wstage[sl], d=wblk[i % 2]:
                       e.tensor_copy(d[:, h2 * 1024:(h2 + 1) * 1024], s[:, :])),
                      reads=[f"wst{sl}"], writes=[f"wblk{i % 2}"],
                      tag=f"{tagp}wc{pt}_{h2}")
            ps, ps_r = PROJ[i % 2], PROJ_R[i % 2]
            fns = []
            for ct in range(16):
                for fc in range(4):
                    fns.append(MM(ps[:, fc * 512:(fc + 1) * 512],
                                  wblk[i % 2][:, ct * 128:(ct + 1) * 128],
                                  src[:, ct * T + fc * 512: ct * T + (fc + 1) * 512],
                                  start=(ct == 0), stop=(ct == 15)))
            reads = [f"wblk{i % 2}"] + sum([src_res(ct) for ct in range(16)], [])
            p.add("tensor", seq_fns(fns), reads=reads, writes=ps_r,
                  tag=f"{tagp}mm{pt}")
            epi_cb(pn, pt, i, ps, ps_r)
            wcnt[0] += 1

    if sub >= 9:
        w_proj_src(wo, "wo", wo_epi, ABUF, lambda ct: ab_tile(ct))

    def ffn(tagp):
        ln_stats(tagp, RES, lambda ft: [f"res{ft}"])
        build_bc(tagp, rows16, "rows16", True)

        def f_epi(pn, pt, i, ps, ps_r):
            for fc in range(4):
                s = slice(fc * 512, (fc + 1) * 512)
                d = slice(pt * T + fc * 512, pt * T + (fc + 1) * 512)
                lnfold_chunk(tagp, pt, fc, ps, ps_r)
                p.add("scalar",
                      (lambda e, s=s, d=d, pt=pt:
                       e.activation(ABUF[:, d], scr1[:, s], AF.Identity,
                                    bias=bcol(FFB, pt), scale=bcol(FFS, pt))),
                      reads=[f"scr1h{fc // 2}", "bias_sb"], writes=ab_chunk(pt, fc),
                      tag=f"{tagp}sb{pt}_{fc}")

        w_proj(ffnw, tagp, f_epi, wsum=True)
        for ft in range(16):
            p.add("vector",
                  (lambda e, ft=ft:
                   e.tensor_add(RES[:, ft * T:(ft + 1) * T],
                                RES[:, ft * T:(ft + 1) * T],
                                ABUF[:, ft * T:(ft + 1) * T])),
                  reads=[f"res{ft}"] + ab_tile(ft), writes=[f"res{ft}"],
                  tag=f"{tagp}radd{ft}")

    if stage >= 2:
        ffn("f1")

    if stage >= 3:
        ln_stats("l3", RES, lambda ft: [f"res{ft}"])
        build_bc("l3", rows16, "rows16", True)

        ld16(rq_c[:, 0:1024], AB_RQC, ropeq_c[:, 0:1024], "rqc0", 1024)
        ld16(rq_c[:, 1024:2048], AB_RQC, ropeq_c[:, 1024:2048], "rqc1", 1024)
        ld16(rq_s[:, 0:1024], AB_RQS, ropeq_s[:, 0:1024], "rqs0", 1024)
        ld16(rq_s[:, 1024:2048], AB_RQS, ropeq_s[:, 1024:2048], "rqs1", 1024)
        ld16(rk_cs[:, 0:1024], AB_RKCS, ropek_in[:, 0:1024], "rkc0", 1024)
        ld16(rk_cs[:, 1024:2048], AB_RKCS, ropek_in[:, 1024:2048], "rkc1", 1024)
        ld16(kexp16[:, :], PL_KEXP, kexp.ap(), "kexp", 1024)
        ld16(vexp16[0:64, 0:1024], PL_VEXP, vexp[0:64, 0:1024], "vexp0", 1024, rows=64)
        ld16(vexp16[0:64, 1024:2048], PL_VEXP, vexp[0:64, 1024:2048], "vexp1", 1024, rows=64)
        ld16(masks[:, 0:1024], PL_MASK, masks_in[:, 0:1024], "mask0", 1024)
        ld16(masks[:, 1024:2048], PL_MASK, masks_in[:, 1024:2048], "mask1", 1024)

        def kvl_epi(pn, pt, i, ps, ps_r):
            for fc in range(4):
                lnfold_chunk("kl", pt, fc, ps, ps_r)
            p.add("vector",
                  (lambda e: e.tensor_copy(agbuf[64:128, :], scr1[64:128, :])),
                  reads=["scr1h0", "scr1h1"], writes=PL_AG, tag="kl_v")
            for hh in range(2):
                hs = slice(hh * 1024, (hh + 1) * 1024)
                fns = [MM(SP[0:64, fc * 512:(fc + 1) * 512], p128[0:64, 0:64],
                          scr1[0:64, hh * 1024 + fc * 512: hh * 1024 + (fc + 1) * 512],
                          start=True, stop=True) for fc in range(2)]
                p.add("tensor", seq_fns(fns), reads=["scr1h0", "scr1h1", "p128"],
                      writes=SP_R, tag=f"kl_sw{hh}")
                p.add("vector",
                      (lambda e, hs=hs: e.tensor_mul(scr2[0:64, :], SP[0:64, :],
                                                     rk_cs[64:128, hs])),
                      reads=SP_R + AB_RKCS, writes=["scr2a"], tag=f"kl_s{hh}")
                p.add("vector",
                      (lambda e, hs=hs: e.tensor_mul(agbuf[0:64, hs], scr1[0:64, hs],
                                                     rk_cs[0:64, hs])),
                      reads=["scr1h0", "scr1h1"] + AB_RKCS, writes=PL_AG,
                      tag=f"kl_c{hh}")
                p.add("vector",
                      (lambda e, hs=hs: e.tensor_add(agbuf[0:64, hs], agbuf[0:64, hs],
                                                     scr2[0:64, :])),
                      reads=["scr2a"] + PL_AG, writes=PL_AG, tag=f"kl_a{hh}")

        w_proj(wkv, "kl", kvl_epi, pts=[0], wsum=True)
        p.add("gpsimd", lambda e: e.dma_start(out=ag_in.ap(), in_=agbuf[:, :]),
              reads=PL_AG, writes=["ag_in"], dsem="agst", tag="ag_store")
        if _mockcc:
            fns2 = [(lambda e, r=r: e.dma_start(out=ag_out[r * 128:(r + 1) * 128, :],
                                                in_=ag_in.ap())) for r in range(4)]
            for r, f2 in enumerate(fns2):
                p.add("gpsimd", f2, reads=["ag_in"], writes=["ag_out"],
                      dsem="cc2", tag=f"allgather_mock{r}")
        else:
            p.add("gpsimd",
                  (lambda e: e.collective_compute(
                      "AllGather", ALU.bypass,
                      replica_groups=[[0, 1, 2, 3], [4, 5, 6, 7]],
                      ins=[ag_in.ap().opt()], outs=[ag_out.ap().opt()])),
                  reads=["ag_in"], writes=["ag_out"], dsem="cc2", dinc=1, tag="allgather")

        def q_epi_for(h):
            def q_epi(pn, pt, i, ps, ps_r):
                for fc in range(4):
                    lnfold_chunk(f"q{h}", pt, fc, ps, ps_r)
                fns = [MM(ps[:, fc * 512:(fc + 1) * 512], p128[:, :],
                          scr1[:, fc * 512:(fc + 1) * 512], start=True, stop=True)
                       for fc in range(4)]
                p.add("tensor", seq_fns(fns), reads=["scr1h0", "scr1h1", "p128"],
                      writes=ps_r, tag=f"q_sw{h}")
                for hh in range(2):
                    hs = slice(hh * 1024, (hh + 1) * 1024)
                    p.add("vector",
                          (lambda e, hs=hs: e.tensor_mul(scr1[:, hs], scr1[:, hs],
                                                         rq_c[:, hs])),
                          reads=[f"scr1h{hh}"] + AB_RQC, writes=[f"scr1h{hh}"],
                          tag=f"q_rc{h}{hh}")
                    p.add("vector",
                          (lambda e, hs=hs, ps=ps: e.tensor_mul(scr2[:, :], ps[:, hs],
                                                                rq_s[:, hs])),
                          reads=ps_r + AB_RQS, writes=["scr2a"], tag=f"q_rs{h}{hh}")
                    p.add("vector",
                          (lambda e, hs=hs: e.tensor_add(scr1[:, hs], scr1[:, hs],
                                                         scr2[:, :])),
                          reads=["scr2a", f"scr1h{hh}"], writes=[f"scr1h{hh}"],
                          tag=f"q_ra{h}{hh}")
                rowh = (h % 2) * 64
                coloff = (h // 8) * 512 + ((h % 8) // 2) * 128
                for g2 in range(2):
                    fns = [MM(SP[0:64, fc * 512:(fc + 1) * 512],
                              kexp16[:, h * 64:(h + 1) * 64],
                              scr1[:, g2 * 1024 + fc * 512: g2 * 1024 + (fc + 1) * 512],
                              start=True, stop=True) for fc in range(2)]
                    p.add("tensor", seq_fns(fns),
                          reads=["scr1h0", "scr1h1"] + PL_KEXP, writes=SP_R,
                          tag=f"q_qt{h}_{g2}")
                    dst = bass.AP(qt_all, rowh * 16384 * 0 + g2 * 8 * 1024 + coloff,
                                  [[16384, 64], [1024, 8], [1, 128]])
                    if rowh:
                        dst = bass.AP(qt_all, g2 * 8 * 1024 + coloff,
                                      [[16384, 64], [1024, 8], [1, 128]])
                        dst.tensor
                    qv = qt_all[rowh:rowh + 64, :].rearrange("p (qt c) -> p qt c", qt=16)
                    p.add("scalar",
                          (lambda e, qv=qv, g2=g2, coloff=coloff:
                           e.activation(qv[:, g2 * 8:(g2 + 1) * 8,
                                           coloff:coloff + 128],
                                        SP[0:64, 0:1024].rearrange(
                                            "p (qt c) -> p qt c", qt=8),
                                        AF.Identity, scale=SCALE)),
                          reads=SP_R, writes=AB_QT, tag=f"q_ev{h}_{g2}")
                    p.add("scalar",
                          (lambda e: e.activation(scr2[0:64, :], SP[0:64, 0:1024],
                                                  AF.Square)),
                          reads=SP_R, writes=["scr2a"], tag=f"q_sq{h}_{g2}")
                    fns = [MM(YB[0:1, g2 * 1024 + fc * 512:
                              g2 * 1024 + (fc + 1) * 512],
                              ones_c16[0:64, 0:1], scr2[0:64, fc * 512:(fc + 1) * 512],
                              start=(h % 2 == 0), stop=False) if False else
                           MM(YB[0:1, fc * 512:(fc + 1) * 512],
                              ones_c16[0:64, 0:1], scr2[0:64, fc * 512:(fc + 1) * 512],
                              start=True, stop=True) for fc in range(2)]
                    p.add("tensor", seq_fns(fns), reads=["scr2a", "ones_c16"],
                          writes=YB_R, tag=f"q_nm{h}_{g2}")
                    p.add("vector",
                          (lambda e, g2=g2:
                           e.reduce_max(m2tmp[0:1, g2 * 8:(g2 + 1) * 8],
                                        YB[0:1, 0:1024].rearrange(
                                            "p (qt c) -> p qt c", qt=8),
                                        axis=mybir.AxisListType.X)),
                          reads=YB_R, writes=["m2tmp"], tag=f"q_mx{h}_{g2}")
                if h == 0:
                    p.add("vector", lambda e: e.tensor_copy(m2row[0:1, :], m2tmp[0:1, :]),
                          reads=["m2tmp"], writes=["m2row"], tag=f"q_m2i{h}")
                else:
                    p.add("vector", lambda e: e.tensor_max(m2row[0:1, :], m2row[0:1, :],
                                                           m2tmp[0:1, :]),
                          reads=["m2tmp", "m2row"], writes=["m2row"], tag=f"q_m2{h}")
            return q_epi

        for h in range(16):
            w_proj(wq, f"q{h}", q_epi_for(h), pts=[h], wsum=True)

        for rowh in range(2):
            p.add("sync",
                  (lambda e, rowh=rowh: e.dma_start(
                      out=k_all[rowh * 64:(rowh + 1) * 64, :].rearrange(
                          "p (r q) -> p r q", r=4),
                      in_=bass.AP(ag_out, 0, [[D, 64], [128 * D, 4], [1, D]]))),
                  reads=["ag_out"], writes=AB_KALL, dsem=f"kba{rowh}",
                  tag=f"kback{rowh}")
        p.add("vector",
              (lambda e: e.memset(bass.AP(v_tok, 64, [[4160, 128], [65, 64], [1, 1]]),
                                  1.0)),
              writes=AB_VTOK, tag="vt_ones")
        for r2 in range(4):
            p.add("sync",
                  (lambda e, r2=r2: e.dma_start(
                      out=scr1[0:64, :],
                      in_=bass.AP(ag_out, (128 * r2 + 64) * D, [[D, 64], [1, D]]))),
                  reads=["ag_out", "scr1h0", "scr1h1"], writes=["scr1h0", "scr1h1"],
                  dsem="vback", tag=f"vback{r2}")
            fns = [(lambda e, m2=m2, r2=r2: e.transpose(
                        PT16[:, m2 * 64:(m2 + 1) * 64],
                        scr1[0:64, m2 * 128:(m2 + 1) * 128], ident16[0:64, 0:64]))
                   for m2 in range(16)]
            p.add("tensor", seq_fns(fns),
                  reads=["scr1h0", "scr1h1", "ident16"], writes=PT16_R,
                  tag=f"vtr{r2}")
            p.add("vector",
                  (lambda e, r2=r2: e.tensor_copy(
                      bass.AP(v_tok, (r2 * 16) * 65, [[4160, 128], [65, 16], [1, 64]]),
                      PT16[:, :].rearrange("p (m c) -> p m c", m=16))),
                  reads=PT16_R, writes=AB_VTOK, tag=f"vte{r2}")

        for c8 in range(8):
            p.add("vector",
                  (lambda e, c8=c8: e.tensor_mul(scr2[0:64, :],
                                                 k_all[0:64, c8 * 1024:(c8 + 1) * 1024],
                                                 k_all[0:64, c8 * 1024:(c8 + 1) * 1024])),
                  reads=AB_KALL, writes=["scr2a"], tag=f"km_sq{c8}")
            fns = [MM(YB[0:1, fc * 512:(fc + 1) * 512], ones_c16[0:64, 0:1],
                      scr2[0:64, fc * 512:(fc + 1) * 512], start=True, stop=True)
                   for fc in range(2)]
            p.add("tensor", seq_fns(fns), reads=["scr2a", "ones_c16"], writes=YB_R,
                  tag=f"km_mm{c8}")
            p.add("vector",
                  (lambda e: e.reduce_max(kmaxc[0:1, 1:2], YB[0:1, 0:1024],
                                          axis=mybir.AxisListType.X)),
                  reads=YB_R, writes=["kmx1"], tag=f"km_rx{c8}")
            if c8 == 0:
                p.add("vector", lambda e: e.tensor_copy(kmaxc[0:1, 0:1], kmaxc[0:1, 1:2]),
                      reads=["kmx1"], writes=["kmx0"], tag=f"km_i{c8}")
            else:
                p.add("vector", lambda e: e.tensor_max(kmaxc[0:1, 0:1], kmaxc[0:1, 0:1],
                                                       kmaxc[0:1, 1:2]),
                      reads=["kmx1", "kmx0"], writes=["kmx0"], tag=f"km_a{c8}")

        p.add("vector", lambda e: e.tensor_scalar_mul(m2row[0:1, :], m2row[0:1, :],
                                                      kmaxc[0:1, 0:1]),
              reads=["m2row", "kmx0"], writes=["m2row"], tag="g_mul")
        p.add("scalar", lambda e: e.activation(m2row[0:1, :], m2row[0:1, :], AF.Sqrt),
              reads=["m2row"], writes=["m2row"], tag="g_sqrt")
        p.add("vector", lambda e: e.tensor_scalar_mul(m2row[0:1, :], m2row[0:1, :],
                                                      -SCALE * SCALE),
              reads=["m2row"], writes=["m2row"], tag="g_neg")
        p.add("tensor", MM(SP[:, 0:16], ones_l32[0:1, :], m2row[0:1, :],
                           start=True, stop=True),
              reads=["m2row", "ones_l32"], writes=SP_R, tag="g_bc")
        p.add("vector", lambda e: e.tensor_copy(gcol[:, :], SP[:, 0:16]),
              reads=SP_R, writes=PL_GCOL, tag="g_ev")

        pwcnt = [0]
        for m in range(16):
            for pp in range(2):
                kbs = [(m2, r2) for m2 in range(m + 1) for r2 in range(4)]
                nkb = len(kbs)
                for j, (m2, r2) in enumerate(kbs):
                    kcol = (r2 * 16 + m2) * 128
                    kt = r2 * 16 + m2
                    ppsum, ppr = PSC[j % 2], PSC_R[j % 2]
                    at = attnT[j % 2]
                    atr = AB_AT[j % 2]
                    fns = [MM(ppsum[:, rh * 512:(rh + 1) * 512],
                              k_all[rh * 64:rh * 64 + 64, kcol:kcol + 128],
                              qt_all[rh * 64:rh * 64 + 64,
                                     m * 1024 + pp * 512: m * 1024 + (pp + 1) * 512],
                              start=True, stop=True) for rh in range(2)]
                    p.add("tensor", seq_fns(fns), reads=AB_KALL + AB_QT,
                          writes=ppr, tag=f"sc{m}_{pp}_{j}")
                    p.add("scalar",
                          (lambda e, at=at, ppsum=ppsum, m=m:
                           e.activation(at[:, :], ppsum[:, :], AF.Exp,
                                        bias=gcol[:, m:m + 1])),
                          reads=ppr + PL_GCOL, writes=atr, tag=f"ex{m}_{pp}_{j}")
                    if m2 == m:
                        for ch in range(2):
                            p.add("vector",
                                  (lambda e, at=at, ch=ch, r2=r2:
                                   e.tensor_mul(at[:, ch * 512:(ch + 1) * 512],
                                                at[:, ch * 512:(ch + 1) * 512],
                                                masks[:, r2 * 512:(r2 + 1) * 512])),
                                  reads=atr + PL_MASK, writes=atr,
                                  tag=f"mk{m}_{pp}_{j}_{ch}")
                    fns = [MM(YB[0:65, ch * 512:(ch + 1) * 512],
                              v_tok[:, kt * 65:(kt + 1) * 65],
                              at[:, ch * 512:(ch + 1) * 512],
                              start=(j == 0), stop=(j == nkb - 1))
                           for ch in range(2)]
                    p.add("tensor", seq_fns(fns), reads=AB_VTOK + atr,
                          writes=YB_R, tag=f"av{m}_{pp}_{j}")
                p.add("vector",
                      (lambda e: e.reciprocal(row32[0:1, 0:1024], YB[64:65, 0:1024])),
                      reads=YB_R, writes=["rw0"], tag=f"fn_rs{m}_{pp}")
                fns = [MM(SP[0:64, fc * 512:(fc + 1) * 512], ones_l32[0:1, 0:64],
                          row32[0:1, fc * 512:(fc + 1) * 512], start=True, stop=True)
                       for fc in range(2)]
                p.add("tensor", seq_fns(fns), reads=["rw0", "ones_l32"], writes=SP_R,
                      tag=f"fn_bc{m}_{pp}")
                p.add("vector",
                      (lambda e: e.tensor_copy(y_sb[0:64, :], YB[0:64, 0:1024])),
                      reads=YB_R, writes=PL_YSB, tag=f"fn_ys{m}_{pp}")
                p.add("vector",
                      (lambda e: e.tensor_mul(y_norm[0:64, :], y_sb[0:64, :],
                                              SP[0:64, 0:1024])),
                      reads=PL_YSB + SP_R, writes=PL_YN, tag=f"fn_yn{m}_{pp}")
                fns = []
                for s in range(8):
                    h = pp * 8 + (2 * s if s < 4 else 2 * (s - 4) + 1)
                    fns.append(MM(SP[:, s * 128:(s + 1) * 128],
                                  vexp16[:, h * 128:(h + 1) * 128],
                                  y_norm[0:64, s * 128:(s + 1) * 128],
                                  start=True, stop=True))
                p.add("tensor", seq_fns(fns), reads=PL_YN + PL_VEXP, writes=SP_R,
                      tag=f"fn_xp{m}_{pp}")
                for gpar in range(2):
                    off = pp * 8 * 512 + gpar * 512 + (m % 4) * 128
                    p.add("vector",
                          (lambda e, off=off, gpar=gpar: e.tensor_copy(
                              bass.AP(ybuf, off, [[8192, 128], [1024, 4], [1, 128]]),
                              bass.AP(SP, gpar * 512, [[1024, 128], [128, 4], [1, 128]]))),
                          reads=SP_R, writes=PL_YBUF, tag=f"fn_sb{m}_{pp}_{gpar}")
            if m % 4 == 3:
                g = m // 4
                for pt in range(16):
                    i = pwcnt[0]
                    for h2 in range(2):
                        sl = h2
                        p.add("sync",
                              (lambda e, pt=pt, h2=h2, s=wstage[sl]:
                               e.dma_start(
                                   out=s[:, :].rearrange("p (c j) -> p c j", c=8),
                                   in_=bass.AP(projw, pt * 128 + h2 * 8 * 128 * D,
                                               [[D, 128], [128 * D, 8], [1, 128]]))),
                              writes=[f"wst{sl}"], dsem=f"wsd{sl}",
                              tag=f"pj{g}wd{pt}_{h2}")
                        p.add("vector",
                              (lambda e, h2=h2, s=wstage[sl], d2=wblk[i % 2]:
                               e.tensor_copy(d2[:, h2 * 1024:(h2 + 1) * 1024], s[:, :])),
                              reads=[f"wst{sl}"], writes=[f"wblk{i % 2}"],
                              tag=f"pj{g}wc{pt}_{h2}")
                    chunk = pt % 2
                    fns = [MM(P0[:, chunk * 512:(chunk + 1) * 512],
                              wblk[i % 2][:, ct * 128:(ct + 1) * 128],
                              ybuf[:, ct * 512:(ct + 1) * 512],
                              start=(ct == 0), stop=(ct == 15)) for ct in range(16)]
                    p.add("tensor", seq_fns(fns),
                          reads=[f"wblk{i % 2}"] + PL_YBUF,
                          writes=[P0_R[chunk]], tag=f"pj{g}mm{pt}")
                    p.add("vector",
                          (lambda e, pt=pt, g=g, chunk=chunk:
                           e.tensor_add(RES[:, pt * T + g * 512: pt * T + (g + 1) * 512],
                                        P0[:, chunk * 512:(chunk + 1) * 512],
                                        RES[:, pt * T + g * 512: pt * T + (g + 1) * 512])),
                          reads=[P0_R[chunk], f"res{pt}"], writes=[f"res{pt}"],
                          tag=f"pj{g}ra{pt}")
                    pwcnt[0] += 1

    if stage >= 4:
        ffn("f2")

    _dump = ABUF if os.environ.get("KDUMP") == "abuf" else RES
    for i in (range(32) if not _nost else range(1)):
        ft, h2 = i // 2, i % 2
        sl = i % 2
        p.add("vector",
              (lambda e, ft=ft, h2=h2, s=wstage[sl]:
               e.tensor_copy(s[:, :], _dump[:, ft * T + h2 * 1024: ft * T + (h2 + 1) * 1024])),
              reads=([f"res{ft}"] if _dump is RES else ab_tile(ft)),
              writes=[f"wst{sl}"], tag=f"ocast{i}")
        p.add("gpsimd",
              (lambda e, ft=ft, h2=h2, s=wstage[sl]:
               e.dma_start(out=out_fm[ft * 128:(ft + 1) * 128,
                                      h2 * 1024:(h2 + 1) * 1024], in_=s[:, :])),
              reads=[f"wst{sl}"], writes=[f"ost{i}"], dsem=f"ost{sl}", tag=f"ostore{i}")
    _dbg = os.environ.get("KDBG")
    if _dbg == "kv16":
        p.add("vector", lambda e: e.tensor_copy(row32[:, :], kv16[:, :]),
              reads=PL_KV16, writes=["rw0", "rw32", "rw64"], tag="dbgcast")
    elif _dbg == "arin":
        p.add("sync", lambda e: e.dma_start(out=row32[:, :], in_=ar_in.ap()),
              reads=["ar_in"], writes=["rw0", "rw32", "rw64"], dsem="dbgl", tag="dbgld")
    if _dbg:
        p.add("gpsimd", lambda e: e.dma_start(out=out_dbg.ap(), in_=row32[:, :]),
              reads=["rw0", "rw32", "rw64"], writes=["dbgo"], dsem="dbgs", tag="dbgst")
    p.add("gpsimd", lambda e: e.engine_nop(),
          reads=([f"ost{i}" for i in (range(32) if not _nost else range(1))]
                 + (["dbgo"] if _dbg else [])), tag="final_wait")

    p.emit()
    return nc


def _host_prep(inputs):
    x = np.asarray(inputs["L0_attn_a_x"])
    maps = []
    ident = np.eye(128, dtype=np.float32)
    p128 = np.zeros((128, 128), dtype=np.float32)
    for i in range(64):
        p128[2 * i, 2 * i + 1] = 1.0
        p128[2 * i + 1, 2 * i] = 1.0
    biasp = np.zeros((128, 128), dtype=np.float32)

    def pack(vec, idx):
        biasp[:, idx * 16:(idx + 1) * 16] = np.asarray(vec).reshape(16, 128).T

    pack(inputs["L0_attn_a_bqk"], 0)
    pack(inputs["L0_attn_a_bv"], 1)
    pack(inputs["L0_attn_a_bu"], 2)
    pack(inputs["L0_attn_a_bo"], 3)
    pack(inputs["L0_ffn_bias"], 4)
    pack(inputs["L0_ffn_scale_weight"], 5)
    pack(inputs["L0_attn_a_ln_g"], 6)
    pack(inputs["L0_attn_a_ln_b"], 7)

    wkv = np.ascontiguousarray(
        np.concatenate([inputs["L0_attn_b_wk_lat"], inputs["L0_attn_b_wv_lat"]],
                       axis=1))
    kexp = np.ascontiguousarray(
        np.asarray(inputs["L0_attn_b_k_expand"]).transpose(2, 0, 1)
        .reshape(128, H * 64))
    vexp = np.ascontiguousarray(
        np.asarray(inputs["L0_attn_b_v_expand"]).transpose(1, 0, 2)
        .reshape(64, H * 128))
    ffnw = np.ascontiguousarray(np.asarray(inputs["L0_ffn_weight"]).T)

    shared = dict(
        wqk=np.ascontiguousarray(inputs["L0_attn_a_Wqk"]),
        wv=np.ascontiguousarray(inputs["L0_attn_a_Wv"]),
        wu=np.ascontiguousarray(inputs["L0_attn_a_Wu"]),
        wo=np.ascontiguousarray(inputs["L0_attn_a_Wo"]),
        ffnw=ffnw,
        wq=np.ascontiguousarray(inputs["L0_attn_b_wq"]),
        wkv=wkv,
        projw=np.ascontiguousarray(inputs["L0_attn_b_proj_w"]),
        kexp=kexp, vexp=vexp, biasp=biasp,
        bvrow=np.asarray(inputs["L0_attn_a_bv"]).reshape(1, D).astype(np.float32),
        ident_in=ident, p128_in=p128,
    )

    for c in range(8):
        B, r = c // 4, c % 4
        qt = [4 * m + r for m in range(16)]
        xr = np.asarray(x[B]).reshape(64, 128, D)[qt].reshape(T, D)
        x_fm = np.ascontiguousarray(xr.T).astype(np.float32)
        pos = (np.arange(T) // 128) * 512 + 128 * r + (np.arange(T) % 128)
        iq = 1.0 / (10000.0 ** (np.arange(0, 128, 2) / 128.0))
        ang = pos[None, :] * iq[:, None]
        c2 = np.repeat(np.cos(ang), 2, axis=0)
        s2 = np.repeat(np.sin(ang), 2, axis=0)
        s2[0::2] *= -1.0
        ik = 1.0 / (10000.0 ** (np.arange(0, 64, 2) / 64.0))
        angk = pos[None, :] * ik[:, None]
        ck = np.repeat(np.cos(angk), 2, axis=0)
        sk = np.repeat(np.sin(angk), 2, axis=0)
        sk[0::2] *= -1.0
        ropek = np.concatenate([ck, sk], axis=0)
        m = np.zeros((128, 4, 4, 128), dtype=np.float32)
        for r2 in range(4):
            if r2 < r:
                m[:, r2] = 1.0
            elif r2 == r:
                tril = (np.arange(128)[:, None] <= np.arange(128)[None, :])
                m[:, r2] = tril[:, None, :].astype(np.float32)
        masks = m.reshape(128, 4 * 512)
        im = dict(shared)
        im.update(x_fm=x_fm,
                  ropeq_c=c2.astype(np.float32), ropeq_s=s2.astype(np.float32),
                  ropek_in=ropek.astype(np.float32),
                  masks_in=np.ascontiguousarray(masks))
        maps.append(im)
    return maps


def kernel(**inputs):
    maps = _host_prep(inputs)
    nc = build(STAGE)
    res = run_bass_kernel_spmd(nc, maps, core_ids=list(range(8)))
    out = np.zeros((2, 8192, 2048), dtype=np.float32)
    for c in range(8):
        B, r = c // 4, c % 4
        qt = [4 * m + r for m in range(16)]
        o = res.results[c]["out_fm"]
        xr = np.ascontiguousarray(o.T).reshape(16, 128, D)
        out[B].reshape(64, 128, D)[qt] = xr
    return out
